# version 02
# baseline (speedup 1.0000x reference)
import numpy as np
import ml_dtypes
from contextlib import ExitStack

import concourse.bass as bass
import concourse.mybir as mybir
from concourse.bass_utils import run_bass_kernel_spmd

F32 = mybir.dt.float32
BF16 = mybir.dt.bfloat16
I32 = mybir.dt.int32
AF = mybir.ActivationFunctionType
ALU = mybir.AluOpType
AX = mybir.AxisListType
BF = ml_dtypes.bfloat16

B, S, D = 2, 4096, 2048
NCORE = 8
TPC = 1024
INTER = 5504
IPAD = 768
EPS = 1e-5


class Sem:
    def __init__(self, ctx, nc, name):
        self.h = ctx.enter_context(nc.semaphore(name))
        self.v = 0

    def inc(self, inst, n=1):
        inst.then_inc(self.h, n)
        self.v += n
        return self.v

    def wait(self, eng, v=None):
        eng.wait_ge(self.h, self.v if v is None else v)



def _np_bf16(x):
    return np.ascontiguousarray(x.astype(BF))


def host_constants(core_id):
    t0 = (core_id % 4) * TPC
    pos = np.arange(t0, t0 + TPC, dtype=np.float32)
    inv_freq = 1.0 / 10000.0 ** (np.arange(0, 32, 2, dtype=np.float32) / 32.0)
    freqs = np.outer(pos, inv_freq)
    cos = np.cos(freqs).astype(np.float32)
    sin = np.sin(freqs).astype(np.float32)
    C = np.ones((128, TPC), np.float32)
    Sg = np.zeros((128, TPC), np.float32)
    for p in range(128):
        c = p % 64
        if c < 16:
            C[p] = cos[:, c]
            Sg[p] = -sin[:, c]
        elif c < 32:
            C[p] = cos[:, c - 16]
            Sg[p] = sin[:, c - 16]
    return {"rope_c": _np_bf16(C), "rope_s": _np_bf16(Sg)}


def cdiv(a, b):
    return (a + b - 1) // b



def build_ghost():
    nc = bass.Bass("TRN2", target_bir_lowering=False)
    gidx = nc.dram_tensor("gidx", [IPAD, D], I32, kind="ExternalInput").ap()
    uidx = nc.dram_tensor("uidx", [IPAD, D], I32, kind="ExternalInput").ap()
    didx = nc.dram_tensor("didx", [256, INTER], I32, kind="ExternalInput").ap()
    lut = nc.dram_tensor("lut", [1, 16], F32, kind="ExternalInput").ap()
    eye = nc.dram_tensor("eye", [128, 128], BF16, kind="ExternalInput").ap()

    wgT = nc.dram_tensor("wgT", [D, IPAD], BF16, kind="ExternalOutput").ap()
    wuT = nc.dram_tensor("wuT", [D, IPAD], BF16, kind="ExternalOutput").ap()
    wdT = nc.dram_tensor("wdT", [INTER, 256], BF16, kind="ExternalOutput").ap()

    with ExitStack() as ctx:
        sb = lambda n, sh, dt=F32: ctx.enter_context(nc.sbuf_tensor(n, sh, dt))
        idx_s = sb("idx_s", [128, INTER], I32)
        idxf = sb("idxf", [128, INTER], BF16)
        lut_s = sb("lut_s", [1, 16])
        lut_bf = sb("lut_bf", [1, 16], BF16)
        ones_r = sb("ones_r", [1, 128], BF16)
        lutb = sb("lutb", [128, 16])
        mbuf = sb("mbuf", [128, 512, 16], BF16)
        wpiece = sb("wpiece", [128, INTER], BF16)
        tpiece = sb("tpiece", [128, 2, 128], BF16)
        eye_s = sb("eye_s", [128, 128], BF16)
        ps_t0 = ctx.enter_context(nc.psum_tensor("ps_t0", [128, 128], BF16))
        ps_t1 = ctx.enter_context(nc.psum_tensor("ps_t1", [128, 128], BF16))
        ps_tp = [ps_t0, ps_t1]
        ps_b = ctx.enter_context(nc.psum_tensor("ps_b", [128, 16], F32))

        s_ld = Sem(ctx, nc, "s_ld")
        s_idx = Sem(ctx, nc, "s_idx")
        s_o0 = Sem(ctx, nc, "s_o0")
        s_o1 = Sem(ctx, nc, "s_o1")
        s_pre = Sem(ctx, nc, "s_pre")
        s_cast = Sem(ctx, nc, "s_cast")
        s_cmp = Sem(ctx, nc, "s_cmp")
        s_tr = Sem(ctx, nc, "s_tr")
        s_ev = Sem(ctx, nc, "s_ev")
        s_out = Sem(ctx, nc, "s_out")
        block = ctx.enter_context(nc.Block())

        jobs = []
        for t in range(6):
            jobs.append(("g", gidx[t * 128:(t + 1) * 128, :], 2048, wgT, t))
        for t in range(6):
            jobs.append(("u", uidx[t * 128:(t + 1) * 128, :], 2048, wuT, t))
        for t in range(2):
            jobs.append(("d", didx[t * 128:(t + 1) * 128, :], INTER, wdT, t))

        def chunks(width):
            out, c = [], 0
            while c < width:
                w = min(512, width - c)
                out.append((c, w))
                c += w
            return out

        @block.sync
        def _(sync):
            sync.dma_start(out=lut_s[:], in_=lut).then_inc(s_ld.h, 16)
            s_ld.v += 16
            sync.dma_start(out=eye_s[:], in_=eye).then_inc(s_ld.h, 16)
            s_ld.v += 16
            for j, (kind, src, width, outap, t) in enumerate(jobs):
                sync.wait_ge(s_cast.h, j)
                sync.dma_start(out=idx_s[:, 0:width], in_=src).then_inc(s_idx.h, 16)
                s_idx.v += 16

        @block.vector
        def _(v):
            v.wait_ge(s_ld.h, 32)
            v.tensor_copy(out=lut_bf[:], in_=lut_s[:])
            i = v.memset(ones_r[:], 1.0)
            s_pre.inc(i)
            v.wait_ge(s_pre.h, 2)
            i = v.tensor_copy(out=lutb[:], in_=ps_b[:, :])
            s_pre.inc(i)
            v.wait_ge(s_pre.h, 3)
            gp = 0
            for j, (kind, src, width, outap, t) in enumerate(jobs):
                v.wait_ge(s_idx.h, 16 * (j + 1))
                i = v.tensor_copy(out=idxf[:, 0:width], in_=idx_s[:, 0:width])
                s_cast.inc(i)
                v.wait_ge(s_cast.h, s_cast.v)
                for ci, (c0, w) in enumerate(chunks(width)):
                    if j > 0 or ci > 0:
                        v.wait_ge(s_cmp.h, s_cmp.v)
                    for k in range(16):
                        i = v.tensor_scalar(
                            out=mbuf[:, 0:w, k], in0=idxf[:, c0:c0 + w],
                            scalar1=float(k), scalar2=lutb[:, k:k + 1],
                            op0=ALU.is_equal, op1=ALU.mult)
                    s_cmp.inc(i)
                    v.wait_ge(s_cmp.h, s_cmp.v)
                    with nc.allow_low_precision(reason="one-hot sum adds exact zeros"):
                        i = v.tensor_reduce(out=wpiece[:, c0:c0 + w],
                                            in_=mbuf[:, 0:w, :], axis=AX.X, op=ALU.add)
                    s_cmp.inc(i)
                npc = cdiv(width, 128)
                for p in range(npc):
                    v.wait_ge(s_tr.h, gp + 1)
                    if gp >= 2:
                        v.wait_ge((s_o0 if gp % 2 == 0 else s_o1).h, 16 * (gp // 2))
                    pw = min(128, width - p * 128)
                    i = v.tensor_copy(out=tpiece[0:pw, gp % 2, :],
                                      in_=ps_tp[gp % 2][0:pw, :])
                    s_ev.inc(i)
                    gp += 1

        @block.tensor
        def _(te):
            te.wait_ge(s_pre.h, 1)
            i = te.matmul(ps_b[:, :], ones_r[0:1, :], lut_bf[0:1, :],
                          start=True, stop=True)
            s_pre.inc(i)
            gp = 0
            cmp_base = 0
            for j, (kind, src, width, outap, t) in enumerate(jobs):
                nch = len(chunks(width))
                cmp_base += 2 * nch
                npc = cdiv(width, 128)
                for p in range(npc):
                    pw = min(128, width - p * 128)
                    te.wait_ge(s_cmp.h, cmp_base)
                    if gp >= 2:
                        te.wait_ge(s_ev.h, gp - 1)
                    i = te.transpose(ps_tp[gp % 2][0:pw, :],
                                     wpiece[:, p * 128:p * 128 + pw], eye_s[:])
                    s_tr.inc(i)
                    gp += 1

        @block.gpsimd
        def _(g):
            gp = 0
            for j, (kind, src, width, outap, t) in enumerate(jobs):
                npc = cdiv(width, 128)
                for p in range(npc):
                    g.wait_ge(s_ev.h, gp + 1)
                    pw = min(128, width - p * 128)
                    dst = outap[p * 128:p * 128 + pw, t * 128:(t + 1) * 128]
                    so = s_o0 if gp % 2 == 0 else s_o1
                    g.dma_start(out=dst, in_=tpiece[0:pw, gp % 2, :]).then_inc(so.h, 16)
                    so.v += 16
                    gp += 1
            g.wait_ge(s_o0.h, s_o0.v)
            g.wait_ge(s_o1.h, s_o1.v)
    return nc


def run_ghost(inputs):
    nc = build_ghost()
    gi = np.asarray(inputs["L0_ffn_gate_idx"])
    ui = np.asarray(inputs["L0_ffn_up_idx"])
    di = np.asarray(inputs["L0_ffn_down_idx"])
    lut = np.asarray(inputs["L0_ffn_lut"]).reshape(1, 16)
    gip = np.zeros((8 * IPAD, D), np.int32)
    gip[:INTER] = gi
    uip = np.zeros((8 * IPAD, D), np.int32)
    uip[:INTER] = ui
    eye = np.eye(128, dtype=BF)
    in_maps = []
    for c in range(NCORE):
        in_maps.append({
            "gidx": np.ascontiguousarray(gip[c * IPAD:(c + 1) * IPAD]),
            "uidx": np.ascontiguousarray(uip[c * IPAD:(c + 1) * IPAD]),
            "didx": np.ascontiguousarray(di[c * 256:(c + 1) * 256]),
            "lut": lut, "eye": eye,
        })
    res = run_bass_kernel_spmd(nc, in_maps, core_ids=list(range(NCORE)))
    wgT = np.concatenate([res.results[c]["wgT"] for c in range(NCORE)], axis=1)[:, :INTER]
    wuT = np.concatenate([res.results[c]["wuT"] for c in range(NCORE)], axis=1)[:, :INTER]
    wdT = np.concatenate([res.results[c]["wdT"] for c in range(NCORE)], axis=1)[:, :256 * NCORE]
    return np.ascontiguousarray(wgT), np.ascontiguousarray(wuT), np.ascontiguousarray(wdT)



def build_qkv(rope, kscale, nhpt):
    nc = bass.Bass("TRN2", target_bir_lowering=False)
    xT = nc.dram_tensor("xT", [D, TPC], F32, kind="ExternalInput").ap()
    w = nc.dram_tensor("w", [D, 3 * D], F32, kind="ExternalInput").ap()
    b_t = nc.dram_tensor("b_t", [128, 48], F32, kind="ExternalInput").ap()
    eyeblk = nc.dram_tensor("eyeblk", [128, nhpt], BF16, kind="ExternalInput").ap()
    if rope:
        rope_c = nc.dram_tensor("rope_c", [128, TPC], BF16, kind="ExternalInput").ap()
        rope_s = nc.dram_tensor("rope_s", [128, TPC], BF16, kind="ExternalInput").ap()
    qT = nc.dram_tensor("qT", [D, TPC], BF16, kind="ExternalOutput").ap()
    kT = nc.dram_tensor("kT", [D, TPC], BF16, kind="ExternalOutput").ap()
    vT = nc.dram_tensor("vT", [D, TPC], BF16, kind="ExternalOutput").ap()
    qn = nc.dram_tensor("qn", [2, 16 * TPC], F32, kind="ExternalOutput").ap()
    kn = nc.dram_tensor("kn", [2, 16 * TPC], F32, kind="ExternalOutput").ap()

    wre = w.rearrange("(dt p) f -> p dt f", p=128)

    with ExitStack() as ctx:
        ctx.enter_context(nc.allow_low_precision(reason="bf16 operand kernel"))
        sb = lambda n, sh, dt=F32: ctx.enter_context(nc.sbuf_tensor(n, sh, dt))
        x_s = sb("x_s", [128, 16, TPC])
        x_bf = sb("x_bf", [128, 16, TPC], BF16)
        wt_f = [sb("wt_f0", [128, 16, 128]), sb("wt_f1", [128, 16, 128])]
        wt_b = [sb("wt_b0", [128, 16, 128], BF16), sb("wt_b1", [128, 16, 128], BF16)]
        qkv_t = [sb("qkv_t0", [128, TPC], BF16), sb("qkv_t1", [128, TPC], BF16)]
        sig_t = sb("sig_t", [128, TPC], BF16)
        tmp_t = sb("tmp_t", [128, TPC], BF16)
        sq_t = sb("sq_t", [128, TPC], BF16)
        nrm_t = [sb("nrm_t0", [2, TPC]), sb("nrm_t1", [2, TPC])]
        b_all = sb("b_all", [128, 48])
        eyeb_s = sb("eyeb_s", [128, nhpt], BF16)
        if rope:
            ropec_s = sb("ropec_s", [128, TPC], BF16)
            ropes_s = sb("ropes_s", [128, TPC], BF16)
        ps_a = ctx.enter_context(nc.psum_tensor("ps_a", [128, 512], F32))
        ps_b = ctx.enter_context(nc.psum_tensor("ps_b", [128, 512], F32))
        ps_s = ctx.enter_context(nc.psum_tensor("ps_s", [2, TPC], F32))

        s_c = Sem(ctx, nc, "s_c")
        s_w = [Sem(ctx, nc, "s_w0"), Sem(ctx, nc, "s_w1")]
        s_cast = Sem(ctx, nc, "s_cast")
        s_pre = Sem(ctx, nc, "s_pre")
        s_mm = Sem(ctx, nc, "s_mm")
        s_evp = Sem(ctx, nc, "s_evp")
        s_rot = [Sem(ctx, nc, "s_rot0"), Sem(ctx, nc, "s_rot1")]
        s_cmb = Sem(ctx, nc, "s_cmb")
        s_sq = Sem(ctx, nc, "s_sq")
        s_nrm = Sem(ctx, nc, "s_nrm")
        s_nev = Sem(ctx, nc, "s_nev")
        s_n = [Sem(ctx, nc, "s_n0"), Sem(ctx, nc, "s_n1")]
        s_q = [Sem(ctx, nc, "s_q0"), Sem(ctx, nc, "s_q1")]
        block = ctx.enter_context(nc.Block())

        NFT = 48
        def ftype(ft):
            return "q" if ft < 16 else ("k" if ft < 32 else "v")
        outmap = {"q": qT, "k": kT, "v": vT}
        nrmmap = {"q": qn, "k": kn}

        class T:
            pass

        @block.sync
        def _(sync):
            sync.dma_start(out=x_s[:], in_=xT.rearrange("(dt p) t -> p dt t", p=128)).then_inc(s_c.h, 16)
            s_c.v += 16
            sync.dma_start(out=b_all[:], in_=b_t).then_inc(s_c.h, 16)
            s_c.v += 16
            sync.dma_start(out=eyeb_s[:], in_=eyeblk).then_inc(s_c.h, 16)
            s_c.v += 16
            if rope:
                sync.dma_start(out=ropec_s[:], in_=rope_c).then_inc(s_c.h, 16)
                s_c.v += 16
                sync.dma_start(out=ropes_s[:], in_=rope_s).then_inc(s_c.h, 16)
                s_c.v += 16
            for ft in range(NFT):
                par = ft % 2
                if ft >= 2:
                    sync.wait_ge(s_cast.h, ft - 1)
                for dq in range(4):
                    sync.dma_start(out=wt_f[par][:, dq * 4:(dq + 1) * 4, :],
                                   in_=wre[:, dq * 4:(dq + 1) * 4, ft * 128:(ft + 1) * 128]
                                   ).then_inc(s_w[par].h, 16)
                    s_w[par].v += 16

        @block.vector
        def _(v):
            v.wait_ge(s_c.h, s_c.v)
            v.tensor_copy(out=x_bf[:], in_=x_s[:])
            i = v.tensor_scalar_mul(out=b_all[:, 16:32], in0=b_all[:, 16:32], scalar1=kscale)
            s_pre.inc(i)
            if rope:
                i = v.memset(sig_t[:], 0.0)
                s_pre.inc(i)
            nqk = 0
            for ft in range(NFT):
                par = ft % 2
                v.wait_ge(s_w[par].h, 64 * (ft // 2 + 1))
                if ft >= 2:
                    v.wait_ge(s_mm.h, ft - 1)
                i = v.tensor_copy(out=wt_b[par][:], in_=wt_f[par][:])
                s_cast.inc(i)
                ty = ftype(ft)
                if rope and ty in ("q", "k"):
                    v.wait_ge(s_evp.h, 2 * (ft + 1))
                    v.wait_ge(s_rot[par].h, s_rot[par].v + 64)
                    s_rot[par].v += 64
                    qt = qkv_t[par]
                    v.tensor_mul(out=tmp_t[:], in0=ropes_s[:], in1=sig_t[:])
                    i = v.tensor_mul(out=qt[:], in0=ropec_s[:], in1=qt[:])
                    s_cmb.inc(i)
                    v.wait_ge(s_cmb.h, s_cmb.v)
                    i = v.tensor_add(out=qt[:], in0=qt[:], in1=tmp_t[:])
                    s_cmb.inc(i)
                if ty in ("q", "k"):
                    if not (rope and ty in ("q", "k")):
                        v.wait_ge(s_evp.h, 2 * (ft + 1))
                    if nqk >= 1:
                        v.wait_ge(s_nrm.h, nqk)
                    i = v.tensor_mul(out=sq_t[:], in0=qkv_t[par][:], in1=qkv_t[par][:])
                    s_sq.inc(i)
                    npar = nqk % 2
                    v.wait_ge(s_nrm.h, nqk + 1)
                    if nqk >= 2:
                        v.wait_ge(s_n[npar].h, 16 * (nqk // 2))
                    i = v.tensor_copy(out=nrm_t[npar][0:nhpt, :], in_=ps_s[0:nhpt, :])
                    s_nev.inc(i)
                    nqk += 1

        @block.tensor
        def _(te):
            te.wait_ge(s_pre.h, 1)
            nqk = 0
            for ft in range(NFT):
                par = ft % 2
                te.wait_ge(s_cast.h, ft + 1)
                if ft >= 1:
                    te.wait_ge(s_evp.h, 2 * ft)
                for dt in range(16):
                    te.matmul(ps_a[:, :], wt_b[par][:, dt, :], x_bf[:, dt, 0:512],
                              start=(dt == 0), stop=(dt == 15))
                    i = te.matmul(ps_b[:, :], wt_b[par][:, dt, :], x_bf[:, dt, 512:TPC],
                                  start=(dt == 0), stop=(dt == 15))
                s_mm.inc(i)
                ty = ftype(ft)
                if ty in ("q", "k"):
                    te.wait_ge(s_sq.h, nqk + 1)
                    if nqk >= 1:
                        te.wait_ge(s_nev.h, nqk)
                    te.matmul(ps_s[0:nhpt, 0:512], eyeb_s[:, :], sq_t[:, 0:512],
                              start=True, stop=True)
                    i = te.matmul(ps_s[0:nhpt, 512:TPC], eyeb_s[:, :], sq_t[:, 512:TPC],
                                  start=True, stop=True)
                    s_nrm.inc(i)
                    nqk += 1

        @block.scalar
        def _(sc):
            nqk = 0
            for ft in range(NFT):
                par = ft % 2
                ty = ftype(ft)
                sc.wait_ge(s_mm.h, ft + 1)
                if ft >= 2:
                    sc.wait_ge(s_q[par].h, 16 * (ft // 2))
                scl = kscale if ty == "k" else 1.0
                sc.activation(out=qkv_t[par][:, 0:512], in_=ps_a[:, :], func=AF.Identity,
                              bias=b_all[:, ft:ft + 1], scale=scl)
                i = sc.activation(out=qkv_t[par][:, 512:TPC], in_=ps_b[:, :], func=AF.Identity,
                                  bias=b_all[:, ft:ft + 1], scale=scl)
                s_evp.inc(i, 2)
                if ty in ("q", "k"):
                    nqk += 1

        @block.gpsimd
        def _(g):
            nqk = 0
            for ft in range(NFT):
                par = ft % 2
                ty = ftype(ft)
                if rope and ty in ("q", "k"):
                    g.wait_ge(s_evp.h, 2 * (ft + 1))
                    if nqk >= 2:
                        g.wait_ge(s_cmb.h, 2 * (nqk - 1))
                    qt = qkv_t[par]
                    for (a, b) in ((0, 16), (16, 0), (64, 80), (80, 64)):
                        g.dma_start(out=sig_t[a:a + 16, :], in_=qt[b:b + 16, :]
                                    ).then_inc(s_rot[par].h, 16)
                        s_rot[par].v += 16
                if ty in ("q", "k") and rope:
                    g.wait_ge(s_cmb.h, 2 * (nqk + 1))
                else:
                    g.wait_ge(s_evp.h, 2 * (ft + 1))
                ftl = ft % 16
                g.dma_start(out=outmap[ty][ftl * 128:(ftl + 1) * 128, :],
                            in_=qkv_t[par][:]).then_inc(s_q[par].h, 16)
                s_q[par].v += 16
                if ty in ("q", "k"):
                    npar = nqk % 2
                    g.wait_ge(s_nev.h, nqk + 1)
                    g.dma_start(out=nrmmap[ty][:, ftl * TPC:(ftl + 1) * TPC],
                                in_=nrm_t[npar][0:2, :]).then_inc(s_n[npar].h, 16)
                    s_n[npar].v += 16
                    nqk += 1
            g.wait_ge(s_q[0].h, s_q[0].v)
            g.wait_ge(s_q[1].h, s_q[1].v)
            g.wait_ge(s_n[0].h, s_n[0].v)
            g.wait_ge(s_n[1].h, s_n[1].v)
    return nc



def build_attn(NHL, DH, raw_out=False):
    nc = bass.Bass("TRN2", target_bir_lowering=False)
    Qh = nc.dram_tensor("Qh", [512, S], BF16, kind="ExternalInput").ap()
    Kh = nc.dram_tensor("Kh", [512, S], BF16, kind="ExternalInput").ap()
    Vh = nc.dram_tensor("Vh", [NHL * S, DH], BF16, kind="ExternalInput").ap()
    qnh = nc.dram_tensor("qnh", [NHL, S], F32, kind="ExternalInput").ap()
    knh = nc.dram_tensor("knh", [NHL, S], F32, kind="ExternalInput").ap()
    t01 = nc.dram_tensor("t01", [128, 128], BF16, kind="ExternalInput").ap()
    eye = nc.dram_tensor("eye", [128, 128], BF16, kind="ExternalInput").ap()
    Oh = nc.dram_tensor("Oh", [512, S], BF16, kind="ExternalOutput").ap()

    l0 = DH == 64
    VW = DH + (1 if l0 else 0)
    NQC = S // 512

    with ExitStack() as ctx:
        ctx.enter_context(nc.allow_low_precision(reason="bf16 operand kernel"))
        sb = lambda n, sh, dt=F32: ctx.enter_context(nc.sbuf_tensor(n, sh, dt))
        kstr = [sb("kstr0", [DH, S], BF16), sb("kstr1", [DH, S], BF16)]
        qstr = [sb("qstr0", [DH, S], BF16), sb("qstr1", [DH, S], BF16)]
        vstr = [sb("vstr0", [128, 32, VW], BF16), sb("vstr1", [128, 32, VW], BF16)]
        ptile = [sb("pt0", [128, 512], BF16), sb("pt1", [128, 512], BF16)]
        otile = [sb("ot0", [DH, 512], BF16), sb("ot1", [DH, 512], BF16)]
        qn_s = sb("qn_s", [NHL, S])
        kn_s = sb("kn_s", [NHL, S])
        red = sb("red", [NHL, 4])
        negMc = sb("negMc", [NHL, 1], BF16)
        negMr = sb("negMr", [1, NHL], BF16)
        negMb = sb("negMb", [128, NHL])
        rden = sb("rden", [1, 512], BF16)
        rdenb = sb("rdenb", [DH, 512], BF16)
        t01_s = sb("t01_s", [128, 128], BF16)
        eye_s = sb("eye_s", [128, 128], BF16)
        ones_r = sb("ones_r", [1, 128], BF16)
        ones_c = sb("ones_c", [128, 1], BF16)
        ps_sc = [ctx.enter_context(nc.psum_tensor("ps_sc0", [128, 512], F32)),
                 ctx.enter_context(nc.psum_tensor("ps_sc1", [128, 512], F32))]
        ps_o = ctx.enter_context(nc.psum_tensor("ps_o", [VW, 512], F32))
        ps_r = ctx.enter_context(nc.psum_tensor("ps_r", [128, 512], F32))
        ps_den = ctx.enter_context(nc.psum_tensor("ps_den", [1, 512], F32))
        ps_m = ctx.enter_context(nc.psum_tensor("ps_m", [1, 128], BF16))

        s_c = Sem(ctx, nc, "s_c")
        s_m1 = Sem(ctx, nc, "s_m1")
        s_str = [Sem(ctx, nc, "s_str0"), Sem(ctx, nc, "s_str1")]
        s_vm = Sem(ctx, nc, "s_vm")
        s_sc = Sem(ctx, nc, "s_sc")
        s_ex = Sem(ctx, nc, "s_ex")
        s_mk = Sem(ctx, nc, "s_mk")
        s_av = Sem(ctx, nc, "s_av")
        s_e1 = Sem(ctx, nc, "s_e1")
        s_bc = Sem(ctx, nc, "s_bc")
        s_e2 = Sem(ctx, nc, "s_e2")
        s_oh = [Sem(ctx, nc, "s_oh0"), Sem(ctx, nc, "s_oh1")]
        block = ctx.enter_context(nc.Block())

        def sched():
            out = []
            for hl in range(NHL):
                for qc in range(NQC):
                    out.append((hl, qc, 4 * (qc + 1)))
            return out
        SCH = sched()

        @block.sync
        def _(sync):
            for ap, t in ((t01, t01_s), (eye, eye_s), (qnh, qn_s), (knh, kn_s)):
                sync.dma_start(out=t[:], in_=ap).then_inc(s_c.h, 16)
                s_c.v += 16

        @block.gpsimd
        def _(g):
            av_after_hl = []
            av = 0
            per = 1 if l0 else 2
            for hl in range(NHL):
                for qc in range(NQC):
                    av += 4 * (qc + 1) * per
                av_after_hl.append(av)

            def stripes(hl):
                par = hl % 2
                g.dma_start(out=kstr[par][:, :], in_=Kh[hl * DH:(hl + 1) * DH, :]
                            ).then_inc(s_str[par].h, 16)
                g.dma_start(out=qstr[par][:, :], in_=Qh[hl * DH:(hl + 1) * DH, :]
                            ).then_inc(s_str[par].h, 16)
                vsrc = Vh[hl * S:(hl + 1) * S, :].rearrange("(st p) d -> p st d", p=128)
                for b4 in range(4):
                    g.dma_start(out=vstr[par][:, b4 * 8:(b4 + 1) * 8, 0:DH],
                                in_=vsrc[:, b4 * 8:(b4 + 1) * 8, :]
                                ).then_inc(s_str[par].h, 16)
                s_str[par].v += 96

            g.wait_ge(s_vm.h, 1)
            stripes(0)
            stripes(1)
            ep = 0
            for (hl, qc, NST) in SCH:
                par = ep % 2
                g.wait_ge(s_e2.h, 2 * ep + 2)
                g.dma_start(out=Oh[hl * DH:(hl + 1) * DH, qc * 512:(qc + 1) * 512],
                            in_=otile[par][:, :]).then_inc(s_oh[par].h, 16)
                s_oh[par].v += 16
                ep += 1
                if qc == NQC - 1 and hl + 2 < NHL:
                    g.wait_ge(s_av.h, av_after_hl[hl])
                    stripes(hl + 2)
            g.wait_ge(s_oh[0].h, s_oh[0].v)
            g.wait_ge(s_oh[1].h, s_oh[1].v)

        @block.vector
        def _(v):
            if l0:
                v.memset(vstr[0][:, :, DH:DH + 1], 1.0)
                i = v.memset(vstr[1][:, :, DH:DH + 1], 1.0)
            else:
                i = v.engine_nop()
            s_vm.inc(i)
            v.wait_ge(s_c.h, s_c.v)
            v.memset(ones_r[:], 1.0)
            v.memset(ones_c[:], 1.0)
            v.reduce_max(out=red[:, 0:1], in_=qn_s[:], axis=AX.X)
            i = v.reduce_max(out=red[:, 1:2], in_=kn_s[:], axis=AX.X)
            s_m1.inc(i)
            v.wait_ge(s_m1.h, 1)
            i = v.tensor_mul(out=red[:, 2:3], in0=red[:, 0:1], in1=red[:, 1:2])
            s_m1.inc(i)
            v.wait_ge(s_m1.h, 3)
            i = v.tensor_scalar_mul(out=negMc[:, :], in0=red[:, 3:4], scalar1=-1.0)
            s_m1.inc(i)
            v.wait_ge(s_m1.h, 5)
            i = v.tensor_copy(out=negMr[:], in_=ps_m[0:1, 0:NHL])
            s_m1.inc(i)
            v.wait_ge(s_m1.h, 7)
            i = v.tensor_copy(out=negMb[:], in_=ps_r[0:128, 0:NHL])
            s_m1.inc(i)
            ep = 0
            mk = 0
            for (hl, qc, NST) in SCH:
                for st in range(4 * qc, NST):
                    Dw = st * 128 - qc * 512
                    v.wait_ge(s_ex.h, _ex_count(SCH, hl, qc, st))
                    i = v.tensor_mul(out=ptile[st % 2][:, Dw:Dw + 128],
                                     in0=ptile[st % 2][:, Dw:Dw + 128], in1=t01_s[:])
                    s_mk.inc(i)
                par = ep % 2
                v.wait_ge(s_av.h, _av_count(SCH, hl, qc, l0))
                if l0:
                    i = v.reciprocal(out=rden[:], in_=ps_o[DH:DH + 1, :])
                else:
                    i = v.reciprocal(out=rden[:], in_=ps_den[0:1, :])
                s_e1.inc(i)
                v.wait_ge(s_bc.h, ep + 1)
                i = v.tensor_copy(out=rdenb[:], in_=ps_r[0:DH, :])
                s_e2.inc(i)
                v.wait_ge(s_e2.h, 2 * ep + 1)
                if ep >= 2:
                    v.wait_ge(s_oh[par].h, 16 * (ep // 2))
                if raw_out:
                    i = v.tensor_copy(out=otile[par][:, :], in_=ps_o[0:DH, :])
                else:
                    i = v.tensor_mul(out=otile[par][:, :], in0=ps_o[0:DH, :], in1=rdenb[:])
                s_e2.inc(i)
                ep += 1

        @block.scalar
        def _(sc):
            sc.wait_ge(s_m1.h, 2)
            i = sc.sqrt(out=red[:, 3:4], in_=red[:, 2:3])
            s_m1.inc(i)
            sc.wait_ge(s_m1.h, 8)
            av = 0
            for (hl, qc, NST) in SCH:
                for st in range(NST):
                    sc.wait_ge(s_sc.h, _sc_count(SCH, hl, qc, st))
                    if _glob_st(SCH, hl, qc, st) >= 2:
                        sc.wait_ge(s_av.h, _av_upto(SCH, hl, qc, st, l0))
                    i = sc.activation(out=ptile[st % 2][:], in_=ps_sc[st % 2][:],
                                      func=AF.Exp, bias=negMb[:, hl:hl + 1], scale=1.0)
                    s_ex.inc(i)

        @block.tensor
        def _(te):
            te.wait_ge(s_c.h, 64)
            te.wait_ge(s_m1.h, 4)
            i = te.transpose(ps_m[0:1, 0:NHL], negMc[:, :], eye_s[0:NHL, 0:NHL])
            s_m1.inc(i)
            te.wait_ge(s_m1.h, 6)
            i = te.matmul(ps_r[0:128, 0:NHL], ones_r[0:1, :], negMr[0:1, :],
                          start=True, stop=True)
            s_m1.inc(i)
            te.wait_ge(s_m1.h, 8)
            ep = 0
            for (hl, qc, NST) in SCH:
                par = hl % 2
                te.wait_ge(s_str[par].h, 96 * (hl // 2 + 1))
                def emit_s(st):
                    if st >= 2:
                        te.wait_ge(s_ex.h, _ex_count(SCH, hl, qc, st - 2))
                    if ep > 0 and st == 0:
                        te.wait_ge(s_e2.h, 2 * ep)
                        te.wait_ge(s_e1.h, ep)
                    i = te.matmul(ps_sc[st % 2][:, :],
                                  kstr[par][:, st * 128:(st + 1) * 128],
                                  qstr[par][:, qc * 512:(qc + 1) * 512],
                                  start=True, stop=True)
                    s_sc.inc(i)

                emit_s(0)
                for stav in range(NST):
                    if stav + 1 < NST:
                        emit_s(stav + 1)
                    te.wait_ge(s_ex.h, _ex_count(SCH, hl, qc, min(stav + 1, NST - 1)))
                    if stav >= 4 * qc:
                        te.wait_ge(s_mk.h, _mk_count(SCH, hl, qc, stav))
                    rlo = max(0, stav * 128 - qc * 512)
                    i = te.matmul(ps_o[:, rlo:512], vstr[par][:, stav, :],
                                  ptile[stav % 2][:, rlo:512],
                                  start=(stav == 0), stop=(stav == NST - 1))
                    s_av.inc(i)
                    if not l0:
                        i = te.matmul(ps_den[:, rlo:512], ones_c[:, 0:1],
                                      ptile[stav % 2][:, rlo:512],
                                      start=(stav == 0), stop=(stav == NST - 1))
                        s_av.inc(i)
                te.wait_ge(s_e1.h, ep + 1)
                i = te.matmul(ps_r[0:DH, :], ones_r[0:1, 0:DH], rden[0:1, :],
                              start=True, stop=True)
                s_bc.inc(i)
                ep += 1
    return nc


def _glob_st(SCH, hl, qc, st):
    g = 0
    for (h2, q2, NST) in SCH:
        if (h2, q2) == (hl, qc):
            return g + st
        g += NST
    raise KeyError


def _sc_count(SCH, hl, qc, st):
    return _glob_st(SCH, hl, qc, st) + 1


def _ex_count(SCH, hl, qc, st):
    return _glob_st(SCH, hl, qc, st) + 1


def _mk_count(SCH, hl, qc, st):
    m = 0
    for (h2, q2, NST) in SCH:
        lo = 4 * q2
        if (h2, q2) == (hl, qc):
            assert st >= lo
            return m + (st - lo) + 1
        m += NST - lo
    raise KeyError


def _av_upto(SCH, hl, qc, st, l0):
    per = 1 if l0 else 2
    c = 0
    for (h2, q2, NST) in SCH:
        if (h2, q2) == (hl, qc):
            return c + per * max(0, st - 1)
        c += per * NST
    raise KeyError


def _av_count(SCH, hl, qc, l0):
    per = 1 if l0 else 2
    c = 0
    for (h2, q2, NST) in SCH:
        c += per * NST
        if (h2, q2) == (hl, qc):
            return c
    raise KeyError



def build_postln(with_bias=True):
    nc = bass.Bass("TRN2", target_bir_lowering=False)
    oT = nc.dram_tensor("oT", [D, TPC], BF16, kind="ExternalInput").ap()
    xT = nc.dram_tensor("xT", [D, TPC], F32, kind="ExternalInput").ap()
    w = nc.dram_tensor("w", [D, D], F32, kind="ExternalInput").ap()
    b_row = nc.dram_tensor("b_row", [1, D], F32, kind="ExternalInput").ap()
    yT = nc.dram_tensor("yT", [D, TPC], F32, kind="ExternalOutput").ap()
    wre = w.rearrange("(dt p) f -> p dt f", p=128)

    with ExitStack() as ctx:
        ctx.enter_context(nc.allow_low_precision(reason="bf16 operand kernel"))
        sb = lambda n, sh, dt=F32: ctx.enter_context(nc.sbuf_tensor(n, sh, dt))
        x_s = sb("x_s", [128, 16, TPC])
        o_s = sb("o_s", [128, 16, TPC], BF16)
        h_bf = sb("h_bf", [128, 16, TPC], BF16)
        wt_f = [sb("wt_f0", [128, 16, 128]), sb("wt_f1", [128, 16, 128])]
        wt_b = [sb("wt_b0", [128, 16, 128], BF16), sb("wt_b1", [128, 16, 128], BF16)]
        sq_t = sb("sq_t", [128, TPC], BF16)
        b_bf = sb("b_bf", [1, D], BF16)
        ones_r = sb("ones_r", [1, 512], BF16)
        ones_c = sb("ones_c", [128, 1], BF16)
        rows = sb("rows", [1, 6, TPC])
        racc = sb("racc", [1, 2, TPC])
        eps_t = sb("eps_t", [1, 1])
        rows_bf = sb("rows_bf", [1, 2, TPC], BF16)
        tmp_f = sb("tmp_f", [128, D])
        ps_a = ctx.enter_context(nc.psum_tensor("ps_a", [128, 512], F32))
        ps_b = ctx.enter_context(nc.psum_tensor("ps_b", [128, 512], F32))
        ps_s = [[ctx.enter_context(nc.psum_tensor(f"ps_s{i}{c}", [1, 512], F32))
                 for c in range(2)] for i in range(2)]
        ps_r = ctx.enter_context(nc.psum_tensor("ps_r", [128, 2, 512], F32))

        s_c = Sem(ctx, nc, "s_c")
        s_w = [Sem(ctx, nc, "s_w0"), Sem(ctx, nc, "s_w1")]
        s_cast = Sem(ctx, nc, "s_cast")
        s_pre = Sem(ctx, nc, "s_pre")
        s_mm = Sem(ctx, nc, "s_mm")
        s_ra = Sem(ctx, nc, "s_ra")
        s_sq = Sem(ctx, nc, "s_sq")
        s_st = Sem(ctx, nc, "s_st")
        s_sa = Sem(ctx, nc, "s_sa")
        s_ln = Sem(ctx, nc, "s_ln")
        s_bc = Sem(ctx, nc, "s_bc")
        s_nm = Sem(ctx, nc, "s_nm")
        s_y = [Sem(ctx, nc, "s_y0"), Sem(ctx, nc, "s_y1")]
        block = ctx.enter_context(nc.Block())

        @block.sync
        def _(sync):
            sync.dma_start(out=x_s[:], in_=xT.rearrange("(dt p) t -> p dt t", p=128)).then_inc(s_c.h, 16)
            s_c.v += 16
            sync.dma_start(out=o_s[:], in_=oT.rearrange("(dt p) t -> p dt t", p=128)).then_inc(s_c.h, 16)
            s_c.v += 16
            sync.dma_start(out=tmp_f[0:1, 0:D], in_=b_row).then_inc(s_c.h, 16)
            s_c.v += 16
            for et in range(16):
                par = et % 2
                if et >= 2:
                    sync.wait_ge(s_cast.h, et - 1)
                for dq in range(4):
                    sync.dma_start(out=wt_f[par][:, dq * 4:(dq + 1) * 4, :],
                                   in_=wre[:, dq * 4:(dq + 1) * 4, et * 128:(et + 1) * 128]
                                   ).then_inc(s_w[par].h, 16)
                    s_w[par].v += 16

        @block.vector
        def _(v):
            v.wait_ge(s_c.h, s_c.v)
            v.memset(ones_r[:], 1.0)
            v.memset(ones_c[:], 1.0)
            v.memset(racc[:], 0.0)
            v.memset(eps_t[:], EPS)
            i = v.tensor_copy(out=b_bf[0:1, :], in_=tmp_f[0:1, 0:D])
            s_pre.inc(i)
            for et in range(2):
                v.wait_ge(s_w[et].h, 64)
                i = v.tensor_copy(out=wt_b[et][:], in_=wt_f[et][:])
                s_cast.inc(i)
            for et in range(16):
                v.wait_ge(s_mm.h, et + 1)
                if et + 2 < 16:
                    par2 = et % 2
                    v.wait_ge(s_w[par2].h, 64 * (et // 2 + 2))
                    i = v.tensor_copy(out=wt_b[par2][:], in_=wt_f[par2][:])
                    s_cast.inc(i)
                v.tensor_add(out=x_s[:, et, 0:512], in0=x_s[:, et, 0:512], in1=ps_a[:, :])
                i = v.tensor_add(out=x_s[:, et, 512:TPC], in0=x_s[:, et, 512:TPC], in1=ps_b[:, :])
                s_ra.inc(i, 2)
                v.wait_ge(s_ra.h, s_ra.v)
                i = v.tensor_copy(out=h_bf[:, et, :], in_=x_s[:, et, :])
                s_ra.inc(i)
                v.wait_ge(s_ra.h, s_ra.v)
                if et >= 1:
                    v.wait_ge(s_st.h, et)
                i = v.tensor_mul(out=sq_t[:], in0=h_bf[:, et, :], in1=h_bf[:, et, :])
                s_sq.inc(i)
                v.wait_ge(s_st.h, et + 1)
                v.wait_ge(s_sa.h, et)
                v.tensor_add(out=racc[0:1, 0, 0:512], in0=racc[0:1, 0, 0:512], in1=ps_s[0][0][:, :])
                v.tensor_add(out=racc[0:1, 0, 512:TPC], in0=racc[0:1, 0, 512:TPC], in1=ps_s[0][1][:, :])
                v.tensor_add(out=racc[0:1, 1, 0:512], in0=racc[0:1, 1, 0:512], in1=ps_s[1][0][:, :])
                i = v.tensor_add(out=racc[0:1, 1, 512:TPC], in0=racc[0:1, 1, 512:TPC], in1=ps_s[1][1][:, :])
                s_sa.inc(i)
            v.wait_ge(s_sa.h, 16)
            v.tensor_scalar_mul(out=rows[:, 0, :], in0=racc[0:1, 0, :], scalar1=1.0 / D)
            i = v.tensor_scalar_mul(out=rows[:, 1, :], in0=racc[0:1, 1, :], scalar1=1.0 / D)
            s_ln.inc(i)
            v.wait_ge(s_ln.h, 2)
            i = v.tensor_sub(out=rows[:, 2, :], in0=rows[:, 1, :], in1=rows[:, 5, :])
            s_ln.inc(i)
            v.wait_ge(s_ln.h, 4)
            i = v.reciprocal(out=rows[:, 3, :], in_=rows[:, 5, :])
            s_ln.inc(i)
            v.wait_ge(s_ln.h, 5)
            i = v.tensor_mul(out=rows[:, 4, :], in0=rows[:, 0, :], in1=rows[:, 3, :])
            s_ln.inc(i)
            v.wait_ge(s_ln.h, 6)
            v.tensor_copy(out=rows_bf[:, 0, :], in_=rows[:, 3, :])
            i = v.tensor_copy(out=rows_bf[:, 1, :], in_=rows[:, 4, :])
            s_ln.inc(i)
            for c in range(2):
                cs = slice(c * 512, (c + 1) * 512)
                v.wait_ge(s_bc.h, c + 1)
                for et in range(16):
                    par = et % 2
                    if c == 1 and et >= 2:
                        v.wait_ge(s_y[par].h, 16 * (et // 2))
                    v.wait_ge(s_nm.h, s_nm.v)
                    i = v.tensor_mul(out=tmp_f[:, 0:512], in0=x_s[:, et, cs], in1=ps_r[:, 0, :])
                    s_nm.inc(i)
                    v.wait_ge(s_nm.h, s_nm.v)
                    i = v.tensor_sub(out=x_s[:, et, cs], in0=tmp_f[:, 0:512], in1=ps_r[:, 1, :])
                    s_nm.inc(i)

        @block.scalar
        def _(sc):
            sc.wait_ge(s_ln.h, 1)
            i = sc.square(out=rows[:, 5, :], in_=rows[:, 0, :])
            s_ln.inc(i)
            sc.wait_ge(s_ln.h, 3)
            i = sc.activation(out=rows[:, 5, :], in_=rows[:, 2, :], func=AF.Sqrt, bias=eps_t[0:1, 0:1], scale=1.0)
            s_ln.inc(i)

        @block.tensor
        def _(te):
            te.wait_ge(s_pre.h, 1)
            for et in range(16):
                par = et % 2
                te.wait_ge(s_cast.h, et + 1)
                if et >= 1:
                    te.wait_ge(s_ra.h, 3 * (et - 1) + 2)
                for dt in range(16):
                    te.matmul(ps_a[:, :], wt_b[par][:, dt, :], o_s[:, dt, 0:512],
                              start=(dt == 0), stop=(not with_bias and dt == 15))
                    i = te.matmul(ps_b[:, :], wt_b[par][:, dt, :], o_s[:, dt, 512:TPC],
                                  start=(dt == 0), stop=(not with_bias and dt == 15))
                if with_bias:
                    te.matmul(ps_a[:, :], b_bf[0:1, et * 128:(et + 1) * 128],
                              ones_r[0:1, :], start=False, stop=True)
                    i = te.matmul(ps_b[:, :], b_bf[0:1, et * 128:(et + 1) * 128],
                                  ones_r[0:1, :], start=False, stop=True)
                s_mm.inc(i)
                te.wait_ge(s_sq.h, et + 1)
                te.wait_ge(s_sa.h, et)
                te.matmul(ps_s[0][0][:, :], ones_c[:, :], h_bf[:, et, 0:512],
                          start=True, stop=True)
                te.matmul(ps_s[0][1][:, :], ones_c[:, :], h_bf[:, et, 512:TPC],
                          start=True, stop=True)
                te.matmul(ps_s[1][0][:, :], ones_c[:, :], sq_t[:, 0:512],
                          start=True, stop=True)
                i = te.matmul(ps_s[1][1][:, :], ones_c[:, :], sq_t[:, 512:TPC],
                              start=True, stop=True)
                s_st.inc(i)
            for c in range(2):
                cs = slice(c * 512, (c + 1) * 512)
                te.wait_ge(s_ln.h, 7)
                if c == 1:
                    te.wait_ge(s_nm.h, 32)
                te.matmul(ps_r[:, 0, :], ones_r[0:1, 0:128], rows_bf[0:1, 0, cs],
                          start=True, stop=True)
                i = te.matmul(ps_r[:, 1, :], ones_r[0:1, 0:128], rows_bf[0:1, 1, cs],
                              start=True, stop=True)
                s_bc.inc(i)

        @block.gpsimd
        def _(g):
            for et in range(16):
                par = et % 2
                g.wait_ge(s_nm.h, 32 + 2 * (et + 1))
                g.dma_start(out=yT[et * 128:(et + 1) * 128, :], in_=x_s[:, et, :]
                            ).then_inc(s_y[par].h, 16)
                s_y[par].v += 16
            g.wait_ge(s_y[0].h, s_y[0].v)
            g.wait_ge(s_y[1].h, s_y[1].v)
    return nc



def build_ffn0():
    nc = bass.Bass("TRN2", target_bir_lowering=False)
    xT = nc.dram_tensor("xT", [D, TPC], F32, kind="ExternalInput").ap()
    wgT = nc.dram_tensor("wgT", [D, INTER], BF16, kind="ExternalInput").ap()
    wuT = nc.dram_tensor("wuT", [D, INTER], BF16, kind="ExternalInput").ap()
    wdT = nc.dram_tensor("wdT", [INTER, D], BF16, kind="ExternalInput").ap()
    yT = nc.dram_tensor("yT", [D, TPC], F32, kind="ExternalOutput").ap()
    wgre = wgT.rearrange("(dt p) o -> p dt o", p=128)
    wure = wuT.rearrange("(dt p) o -> p dt o", p=128)
    wdre = wdT.rearrange("(ot p) e -> p ot e", p=128)
    NOT = INTER // 128

    with ExitStack() as ctx:
        ctx.enter_context(nc.allow_low_precision(reason="bf16 operand kernel"))
        sb = lambda n, sh, dt=F32: ctx.enter_context(nc.sbuf_tensor(n, sh, dt))
        x_s = sb("x_s", [128, 16, TPC])
        x_bf = sb("x_bf", [128, 16, TPC], BF16)
        ones_cf = None
        hbuf = sb("hbuf", [128, NOT, 512], BF16)
        wg_t = [sb("wg_t0", [128, 16, 128], BF16), sb("wg_t1", [128, 16, 128], BF16)]
        wu_t = [sb("wu_t0", [128, 16, 128], BF16), sb("wu_t1", [128, 16, 128], BF16)]
        wd_t0 = sb("wd_t0", [128, NOT, 128], BF16)
        wd_t = [wd_t0, wd_t0]
        g_s = sb("g_s", [128, 512], BF16)
        sq_t = sb("sq_t", [128, 512], BF16)
        ones_r = sb("ones_r", [1, 512], BF16)
        ones_c = sb("ones_c", [128, 1], BF16)
        ones_cf = sb("ones_cf", [128, 1])
        rows = sb("rows", [1, 6, TPC])
        racc = sb("racc", [1, 2, TPC])
        eps_t = sb("eps_t", [1, 1])
        rows_bf = sb("rows_bf", [1, 2, TPC], BF16)
        tmp_f = sb("tmp_f", [128, 512])
        ps_g = ctx.enter_context(nc.psum_tensor("ps_g", [128, 512], F32))
        ps_u = ctx.enter_context(nc.psum_tensor("ps_u", [128, 512], F32))
        ps_s1 = ctx.enter_context(nc.psum_tensor("ps_s1", [1, 512], F32))
        ps_s2 = ctx.enter_context(nc.psum_tensor("ps_s2", [1, 512], F32))
        ps_r = ctx.enter_context(nc.psum_tensor("ps_r", [128, 2, 512], F32))

        s_c = Sem(ctx, nc, "s_c")
        s_wg = [Sem(ctx, nc, "s_wg0"), Sem(ctx, nc, "s_wg1")]
        s_wu = [Sem(ctx, nc, "s_wu0"), Sem(ctx, nc, "s_wu1")]
        _swd = Sem(ctx, nc, "s_wd")
        s_wd = [_swd, _swd]
        s_pre = Sem(ctx, nc, "s_pre")
        s_gm = Sem(ctx, nc, "s_gm")
        s_si = Sem(ctx, nc, "s_si")
        s_gu = Sem(ctx, nc, "s_gu")
        s_dm = Sem(ctx, nc, "s_dm")
        s_ra = Sem(ctx, nc, "s_ra")
        s_sq = Sem(ctx, nc, "s_sq")
        s_st = Sem(ctx, nc, "s_st")
        s_sa = Sem(ctx, nc, "s_sa")
        s_ln = Sem(ctx, nc, "s_ln")
        s_bc = Sem(ctx, nc, "s_bc")
        s_nm = Sem(ctx, nc, "s_nm")
        s_y = [Sem(ctx, nc, "s_y0"), Sem(ctx, nc, "s_y1")]
        block = ctx.enter_context(nc.Block())

        def wslab(sync, which, par, ot_or_et, sem):
            src = {"g": wgre, "u": wure, "d": wdre}[which]
            dst = {"g": wg_t, "u": wu_t, "d": wd_t}[which][par]
            nd = dst.shape[1]
            q = nd // 4
            for dq in range(4):
                hi = (dq + 1) * q if dq < 3 else nd
                sync.dma_start(out=dst[:, dq * q:hi, :],
                               in_=src[:, dq * q:hi, ot_or_et * 128:(ot_or_et + 1) * 128]
                               ).then_inc(sem[par].h, 16)
                sem[par].v += 16

        @block.sync
        def _(sync):
            sync.dma_start(out=x_s[:], in_=xT.rearrange("(dt p) t -> p dt t", p=128)).then_inc(s_c.h, 16)
            s_c.v += 16
            for tc in range(2):
                for ot in range(NOT):
                    gi = tc * NOT + ot
                    par = gi % 2
                    if gi >= 2:
                        sync.wait_ge(s_gm.h, gi - 1)
                    wslab(sync, "g", par, ot, s_wg)
                    wslab(sync, "u", par, ot, s_wu)
                for et in range(16):
                    de = tc * 16 + et
                    par = 0
                    if de >= 1:
                        sync.wait_ge(s_dm.h, de)
                    wslab(sync, "d", par, et, s_wd)

        @block.vector
        def _(v):
            v.wait_ge(s_c.h, s_c.v)
            v.memset(ones_r[:], 1.0)
            v.memset(ones_c[:], 1.0)
            v.memset(ones_cf[:], 1.0)
            v.memset(racc[:], 0.0)
            v.memset(eps_t[:], EPS)
            i = v.tensor_copy(out=x_bf[:], in_=x_s[:])
            s_pre.inc(i)
            for tc in range(2):
                cs = slice(tc * 512, (tc + 1) * 512)
                for ot in range(NOT):
                    gi = tc * NOT + ot
                    v.wait_ge(s_si.h, gi + 1)
                    i = v.tensor_mul(out=hbuf[:, ot, :], in0=g_s[:, :], in1=ps_u[:, :])
                    s_gu.inc(i)
                for et in range(16):
                    de = tc * 16 + et
                    v.wait_ge(s_dm.h, de + 1)
                    i = v.tensor_add(out=x_s[:, et, cs], in0=x_s[:, et, cs], in1=ps_g[:, :])
                    s_ra.inc(i)
                    v.wait_ge(s_ra.h, s_ra.v)
                    if de >= 1:
                        v.wait_ge(s_st.h, de)
                    i = v.tensor_mul(out=sq_t[:, :], in0=x_s[:, et, cs], in1=x_s[:, et, cs])
                    s_sq.inc(i)
                    v.wait_ge(s_st.h, de + 1)
                    v.wait_ge(s_sa.h, de)
                    v.tensor_add(out=racc[0:1, 0, cs], in0=racc[0:1, 0, cs], in1=ps_s1[:, :])
                    i = v.tensor_add(out=racc[0:1, 1, cs], in0=racc[0:1, 1, cs], in1=ps_s2[:, :])
                    s_sa.inc(i)
            v.wait_ge(s_sa.h, 32)
            v.tensor_scalar_mul(out=rows[:, 0, :], in0=racc[0:1, 0, :], scalar1=1.0 / D)
            i = v.tensor_scalar_mul(out=rows[:, 1, :], in0=racc[0:1, 1, :], scalar1=1.0 / D)
            s_ln.inc(i)
            v.wait_ge(s_ln.h, 2)
            i = v.tensor_sub(out=rows[:, 2, :], in0=rows[:, 1, :], in1=rows[:, 5, :])
            s_ln.inc(i)
            v.wait_ge(s_ln.h, 4)
            i = v.reciprocal(out=rows[:, 3, :], in_=rows[:, 5, :])
            s_ln.inc(i)
            v.wait_ge(s_ln.h, 5)
            i = v.tensor_mul(out=rows[:, 4, :], in0=rows[:, 0, :], in1=rows[:, 3, :])
            s_ln.inc(i)
            v.wait_ge(s_ln.h, 6)
            v.tensor_copy(out=rows_bf[:, 0, :], in_=rows[:, 3, :])
            i = v.tensor_copy(out=rows_bf[:, 1, :], in_=rows[:, 4, :])
            s_ln.inc(i)
            for c in range(2):
                cs = slice(c * 512, (c + 1) * 512)
                v.wait_ge(s_bc.h, c + 1)
                for et in range(16):
                    par = et % 2
                    if c == 1 and et >= 2:
                        v.wait_ge(s_y[par].h, 16 * (et // 2))
                    v.wait_ge(s_nm.h, s_nm.v)
                    i = v.tensor_mul(out=tmp_f[:, :], in0=x_s[:, et, cs], in1=ps_r[:, 0, :])
                    s_nm.inc(i)
                    v.wait_ge(s_nm.h, s_nm.v)
                    i = v.tensor_sub(out=x_s[:, et, cs], in0=tmp_f[:, :], in1=ps_r[:, 1, :])
                    s_nm.inc(i)

        @block.scalar
        def _(sc):
            for tc in range(2):
                cs = slice(tc * 512, (tc + 1) * 512)
                for ot in range(NOT):
                    gi = tc * NOT + ot
                    sc.wait_ge(s_gm.h, gi + 1)
                    if gi >= 1:
                        sc.wait_ge(s_gu.h, gi)
                    i = sc.activation(out=g_s[:, :], in_=ps_g[:, :], func=AF.Silu,
                                      bias=0.0, scale=1.0)
                    s_si.inc(i)
            sc.wait_ge(s_ln.h, 1)
            i = sc.square(out=rows[:, 5, :], in_=rows[:, 0, :])
            s_ln.inc(i)
            sc.wait_ge(s_ln.h, 3)
            i = sc.activation(out=rows[:, 5, :], in_=rows[:, 2, :], func=AF.Sqrt,
                              bias=eps_t[0:1, 0:1], scale=1.0)
            s_ln.inc(i)

        @block.tensor
        def _(te):
            te.wait_ge(s_pre.h, 1)
            for tc in range(2):
                cs = slice(tc * 512, (tc + 1) * 512)
                for ot in range(NOT):
                    gi = tc * NOT + ot
                    par = gi % 2
                    te.wait_ge(s_wg[par].h, 64 * (gi // 2 + 1))
                    te.wait_ge(s_wu[par].h, 64 * (gi // 2 + 1))
                    if gi >= 1:
                        te.wait_ge(s_si.h, gi)
                        te.wait_ge(s_gu.h, gi)
                    for dt in range(16):
                        te.matmul(ps_g[:, :], wg_t[par][:, dt, :], x_bf[:, dt, cs],
                                  start=(dt == 0), stop=(dt == 15))
                        i = te.matmul(ps_u[:, :], wu_t[par][:, dt, :], x_bf[:, dt, cs],
                                      start=(dt == 0), stop=(dt == 15))
                    s_gm.inc(i)
                for et in range(16):
                    de = tc * 16 + et
                    par = 0
                    te.wait_ge(s_wd[par].h, 64 * (de + 1))
                    te.wait_ge(s_gu.h, (tc + 1) * NOT)
                    if de >= 1:
                        te.wait_ge(s_ra.h, de)
                    for ot in range(NOT):
                        i = te.matmul(ps_g[:, :], wd_t[par][:, ot, :], hbuf[:, ot, :],
                                      start=(ot == 0), stop=(ot == NOT - 1))
                    s_dm.inc(i)
                    te.wait_ge(s_sq.h, de + 1)
                    te.wait_ge(s_sa.h, de)
                    te.matmul(ps_s1[:, :], ones_cf[:, :], x_s[:, et, cs],
                              start=True, stop=True)
                    i = te.matmul(ps_s2[:, :], ones_c[:, :], sq_t[:, :],
                                  start=True, stop=True)
                    s_st.inc(i)
            for c in range(2):
                cs = slice(c * 512, (c + 1) * 512)
                te.wait_ge(s_ln.h, 7)
                if c == 1:
                    te.wait_ge(s_nm.h, 32)
                te.matmul(ps_r[:, 0, :], ones_r[0:1, 0:128], rows_bf[0:1, 0, cs],
                          start=True, stop=True)
                i = te.matmul(ps_r[:, 1, :], ones_r[0:1, 0:128], rows_bf[0:1, 1, cs],
                              start=True, stop=True)
                s_bc.inc(i)

        @block.gpsimd
        def _(g):
            for et in range(16):
                par = et % 2
                g.wait_ge(s_nm.h, 32 + 2 * (et + 1))
                g.dma_start(out=yT[et * 128:(et + 1) * 128, :], in_=x_s[:, et, :]
                            ).then_inc(s_y[par].h, 16)
                s_y[par].v += 16
            g.wait_ge(s_y[0].h, s_y[0].v)
            g.wait_ge(s_y[1].h, s_y[1].v)
    return nc



def build_ffn1():
    nc = bass.Bass("TRN2", target_bir_lowering=False)
    xT = nc.dram_tensor("xT", [D, TPC], F32, kind="ExternalInput").ap()
    w = nc.dram_tensor("w", [D, D], F32, kind="ExternalInput").ap()
    apadT = nc.dram_tensor("apadT", [D, 256], F32, kind="ExternalInput").ap()
    bstack = nc.dram_tensor("bstack", [256, D], F32, kind="ExternalInput").ap()
    rwT = nc.dram_tensor("rwT", [D, 8], F32, kind="ExternalInput").ap()
    Rp = nc.dram_tensor("Rp", [8, 256], BF16, kind="ExternalInput").ap()
    eye = nc.dram_tensor("eye", [128, 128], BF16, kind="ExternalInput").ap()
    yT = nc.dram_tensor("yT", [D, TPC], F32, kind="ExternalOutput").ap()
    wre = w.rearrange("(dt p) f -> p dt f", p=128)

    with ExitStack() as ctx:
        ctx.enter_context(nc.allow_low_precision(reason="bf16 operand kernel"))
        sb = lambda n, sh, dt=F32: ctx.enter_context(nc.sbuf_tensor(n, sh, dt))
        x_s = sb("x_s", [128, 16, TPC])
        x_bf = sb("x_bf", [128, 16, TPC], BF16)
        wt_f = [sb("wt_f0", [128, 16, 128]), sb("wt_f1", [128, 16, 128])]
        wt_b = [sb("wt_b0", [128, 16, 128], BF16), sb("wt_b1", [128, 16, 128], BF16)]
        apad_bf = sb("apad_bf", [128, 16, 256], BF16)
        bst_bf = sb("bst_bf", [128, 2, D], BF16)
        rw_f = sb("rw_f", [128, 16, 8])
        rw_bf = sb("rw_bf", [128, 16, 8], BF16)
        rp_s = sb("rp_s", [8, 256], BF16)
        eye_s = sb("eye_s", [128, 128], BF16)
        lg_bf = sb("lg_bf", [8, TPC], BF16)
        ltok = sb("ltok", [128, 8], BF16)
        l2tok = sb("l2tok", [128, 8], BF16)
        m12 = sb("m12", [128, 4])
        v12 = sb("v12", [128, 2])
        mk1 = sb("mk1", [128, 8], BF16)
        mk2 = sb("mk2", [128, 8], BF16)
        wtok = sb("wtok", [128, 8], BF16)
        w8 = sb("w8", [8, TPC], BF16)
        wrep = sb("wrep", [128, 2, TPC], BF16)
        gbuf = sb("gbuf", [128, 2, TPC], BF16)
        sq_t = sb("sq_t", [128, TPC], BF16)
        ones_r = sb("ones_r", [1, 512], BF16)
        ones_c = sb("ones_c", [128, 1], BF16)
        ones_cf = sb("ones_cf", [128, 1])
        rows = sb("rows", [1, 6, TPC])
        racc = sb("racc", [1, 2, TPC])
        eps_t = sb("eps_t", [1, 1])
        rows_bf = sb("rows_bf", [1, 2, TPC], BF16)
        tmp_f = sb("tmp_f", [128, 512])
        ps_a = ctx.enter_context(nc.psum_tensor("ps_a", [128, 512], F32))
        ps_b = ctx.enter_context(nc.psum_tensor("ps_b", [128, 512], F32))
        ps_t1 = ctx.enter_context(nc.psum_tensor("ps_t1", [128, 8], BF16))
        ps_t2 = ctx.enter_context(nc.psum_tensor("ps_t2", [8, 128], BF16))
        ps_s1 = ctx.enter_context(nc.psum_tensor("ps_s1", [1, 512], F32))
        ps_s2 = ctx.enter_context(nc.psum_tensor("ps_s2", [1, 512], F32))
        ps_r = ctx.enter_context(nc.psum_tensor("ps_r", [128, 2, 512], F32))

        s_c = Sem(ctx, nc, "s_c")
        s_pp = Sem(ctx, nc, "s_pp")
        s_tpe = Sem(ctx, nc, "s_tpe")
        s_tpv = Sem(ctx, nc, "s_tpv")
        s_w = [Sem(ctx, nc, "s_w0"), Sem(ctx, nc, "s_w1")]
        s_cast = Sem(ctx, nc, "s_cast")
        s_mm = Sem(ctx, nc, "s_mm")
        s_ra = Sem(ctx, nc, "s_ra")
        s_sq = Sem(ctx, nc, "s_sq")
        s_st = Sem(ctx, nc, "s_st")
        s_sa = Sem(ctx, nc, "s_sa")
        s_ln = Sem(ctx, nc, "s_ln")
        s_bc = Sem(ctx, nc, "s_bc")
        s_nm = Sem(ctx, nc, "s_nm")
        s_y = [Sem(ctx, nc, "s_y0"), Sem(ctx, nc, "s_y1")]
        block = ctx.enter_context(nc.Block())

        TOPK_END = 6 + 80
        PRE_MAIN = TOPK_END + 4 + 8

        @block.sync
        def _(sync):
            sync.dma_start(out=x_s[:], in_=xT.rearrange("(dt p) t -> p dt t", p=128)).then_inc(s_c.h, 16)
            sync.dma_start(out=rw_f[:], in_=rwT.rearrange("(dt p) e -> p dt e", p=128)).then_inc(s_c.h, 16)
            sync.dma_start(out=rp_s[:], in_=Rp).then_inc(s_c.h, 16)
            sync.dma_start(out=eye_s[:], in_=eye).then_inc(s_c.h, 16)
            apre = apadT.rearrange("(dt p) e -> p dt e", p=128)
            wtv = [wt_f[j][:].rearrange("p a b -> p (a b)").rearrange(
                "p (c d) -> p c d", d=256) for j in range(2)]
            sync.dma_start(out=wtv[0], in_=apre[:, 0:8, :]).then_inc(s_c.h, 16)
            sync.dma_start(out=wtv[1], in_=apre[:, 8:16, :]).then_inc(s_c.h, 16)
            s_c.v += 96
            sync.wait_ge(s_pp.h, 2)
            bre = bstack.rearrange("(bt p) o -> p bt o", p=128)
            sync.dma_start(out=wt_f[0][:].rearrange("p a b -> p (a b)"),
                           in_=bre[:, 0, :]).then_inc(s_c.h, 16)
            sync.dma_start(out=wt_f[1][:].rearrange("p a b -> p (a b)"),
                           in_=bre[:, 1, :]).then_inc(s_c.h, 16)
            s_c.v += 32
            for et in range(16):
                par = et % 2
                if et < 2:
                    sync.wait_ge(s_pp.h, 4)
                else:
                    sync.wait_ge(s_cast.h, et - 1)
                for dq in range(4):
                    sync.dma_start(out=wt_f[par][:, dq * 4:(dq + 1) * 4, :],
                                   in_=wre[:, dq * 4:(dq + 1) * 4, et * 128:(et + 1) * 128]
                                   ).then_inc(s_w[par].h, 16)
                    s_w[par].v += 16

        @block.vector
        def _(v):
            v.wait_ge(s_c.h, 96)
            v.memset(ones_r[:], 1.0)
            v.memset(ones_c[:], 1.0)
            v.memset(ones_cf[:], 1.0)
            v.memset(racc[:], 0.0)
            v.memset(eps_t[:], EPS)
            v.tensor_copy(out=x_bf[:], in_=x_s[:])
            v.tensor_copy(out=rw_bf[:], in_=rw_f[:])
            i = v.tensor_copy(out=apad_bf[:, 0:8, :].rearrange("p a b -> p (a b)"),
                              in_=wt_f[0][:].rearrange("p a b -> p (a b)"))
            i.then_inc(s_pp.h, 1)
            i = v.tensor_copy(out=apad_bf[:, 8:16, :].rearrange("p a b -> p (a b)"),
                              in_=wt_f[1][:].rearrange("p a b -> p (a b)"))
            i.then_inc(s_pp.h, 1)
            v.wait_ge(s_c.h, 128)
            v.tensor_scalar_mul(out=bst_bf[:, 0, :],
                                in0=wt_f[0][:].rearrange("p a b -> p (a b)"), scalar1=2.0)
            i = v.tensor_scalar_mul(out=bst_bf[:, 1, :],
                                    in0=wt_f[1][:].rearrange("p a b -> p (a b)"), scalar1=2.0)
            i.then_inc(s_pp.h, 1)
            v.wait_ge(s_pp.h, 3)
            v.engine_nop().then_inc(s_pp.h, 1)
            v.wait_ge(s_pp.h, 5)
            v.tensor_copy(out=lg_bf[:, 0:512], in_=ps_a[0:8, :])
            i = v.tensor_copy(out=lg_bf[:, 512:TPC], in_=ps_b[0:8, :])
            i.then_inc(s_pp.h, 1)
            for tt in range(8):
                B0 = 6 + 10 * tt

                def step(idx, inst):
                    inst.then_inc(s_pp.h, 1)
                    return B0 + idx
                v.wait_ge(s_tpe.h, 2 * tt + 1)
                i = v.tensor_copy(out=ltok[:], in_=ps_t1[:, :])
                i.then_inc(s_tpv.h, 1)
                v.wait_ge(s_tpv.h, 2 * tt + 1)
                step(1, v.reduce_max(out=m12[:, 0:1], in_=ltok[:], axis=AX.X))
                v.wait_ge(s_pp.h, B0 + 1)
                step(2, v.tensor_scalar(out=mk1[:], in0=ltok[:], scalar1=m12[:, 0:1],
                                        scalar2=None, op0=ALU.is_equal))
                v.wait_ge(s_pp.h, B0 + 2)
                step(3, v.scalar_tensor_tensor(out=l2tok[:], in0=mk1[:], scalar=-30000.0,
                                               in1=ltok[:], op0=ALU.mult, op1=ALU.add))
                v.wait_ge(s_pp.h, B0 + 3)
                step(4, v.reduce_max(out=m12[:, 1:2], in_=l2tok[:], axis=AX.X))
                v.wait_ge(s_pp.h, B0 + 4)
                step(5, v.tensor_sub(out=m12[:, 2:3], in0=m12[:, 0:1], in1=m12[:, 1:2]))
                v.wait_ge(s_pp.h, B0 + 6)
                step(7, v.tensor_scalar(out=v12[:, 1:2], in0=v12[:, 0:1], scalar1=-1.0,
                                        scalar2=1.0, op0=ALU.mult, op1=ALU.add))
                v.wait_ge(s_pp.h, B0 + 7)
                step(8, v.tensor_scalar(out=mk2[:], in0=l2tok[:], scalar1=m12[:, 1:2],
                                        scalar2=None, op0=ALU.is_equal))
                v.wait_ge(s_pp.h, B0 + 8)
                step(9, v.tensor_scalar(out=mk1[:], in0=mk1[:], scalar1=v12[:, 0:1],
                                        scalar2=None, op0=ALU.mult))
                v.wait_ge(s_pp.h, B0 + 9)
                step(10, v.scalar_tensor_tensor(out=wtok[:], in0=mk2[:], scalar=v12[:, 1:2],
                                                in1=mk1[:], op0=ALU.mult, op1=ALU.add))
                v.wait_ge(s_tpe.h, 2 * tt + 2)
                i = v.tensor_copy(out=w8[:, tt * 128:(tt + 1) * 128], in_=ps_t2[:, :])
                i.then_inc(s_tpv.h, 1)
            v.wait_ge(s_pp.h, TOPK_END + 1)
            v.tensor_copy(out=wrep[:, 0, 0:512], in_=ps_a[:, :])
            i = v.tensor_copy(out=wrep[:, 1, 0:512], in_=ps_b[:, :])
            i.then_inc(s_pp.h, 1)
            v.wait_ge(s_pp.h, TOPK_END + 3)
            v.tensor_copy(out=wrep[:, 0, 512:TPC], in_=ps_a[:, :])
            i = v.tensor_copy(out=wrep[:, 1, 512:TPC], in_=ps_b[:, :])
            i.then_inc(s_pp.h, 1)
            k = 0
            for bt in range(2):
                for c in range(2):
                    cs = slice(c * 512, (c + 1) * 512)
                    v.wait_ge(s_pp.h, TOPK_END + 4 + 2 * k + 1)
                    psx = ps_a if c == 0 else ps_b
                    i = v.tensor_mul(out=gbuf[:, bt, cs], in0=psx[:, :], in1=wrep[:, bt, cs])
                    i.then_inc(s_pp.h, 1)
                    k += 1
            for et in range(2):
                v.wait_ge(s_w[et].h, 64)
                i = v.tensor_copy(out=wt_b[et][:], in_=wt_f[et][:])
                s_cast.inc(i)
            for et in range(16):
                v.wait_ge(s_mm.h, et + 1)
                if et + 2 < 16:
                    par2 = et % 2
                    v.wait_ge(s_w[par2].h, 64 * (et // 2 + 2))
                    i = v.tensor_copy(out=wt_b[par2][:], in_=wt_f[par2][:])
                    s_cast.inc(i)
                v.tensor_add(out=x_s[:, et, 0:512], in0=x_s[:, et, 0:512], in1=ps_a[:, :])
                i = v.tensor_add(out=x_s[:, et, 512:TPC], in0=x_s[:, et, 512:TPC], in1=ps_b[:, :])
                s_ra.inc(i)
                v.wait_ge(s_ra.h, s_ra.v)
                if et >= 1:
                    v.wait_ge(s_st.h, 2 * et)
                i = v.tensor_mul(out=sq_t[:, :], in0=x_s[:, et, :], in1=x_s[:, et, :])
                s_sq.inc(i)
                for c in range(2):
                    cs = slice(c * 512, (c + 1) * 512)
                    v.wait_ge(s_st.h, 2 * et + c + 1)
                    v.wait_ge(s_sa.h, s_sa.v)
                    v.tensor_add(out=racc[0:1, 0, cs], in0=racc[0:1, 0, cs], in1=ps_s1[:, :])
                    i = v.tensor_add(out=racc[0:1, 1, cs], in0=racc[0:1, 1, cs], in1=ps_s2[:, :])
                    s_sa.inc(i)
            v.wait_ge(s_sa.h, 32)
            v.tensor_scalar_mul(out=rows[:, 0, :], in0=racc[0:1, 0, :], scalar1=1.0 / D)
            i = v.tensor_scalar_mul(out=rows[:, 1, :], in0=racc[0:1, 1, :], scalar1=1.0 / D)
            s_ln.inc(i)
            v.wait_ge(s_ln.h, 2)
            i = v.tensor_sub(out=rows[:, 2, :], in0=rows[:, 1, :], in1=rows[:, 5, :])
            s_ln.inc(i)
            v.wait_ge(s_ln.h, 4)
            i = v.reciprocal(out=rows[:, 3, :], in_=rows[:, 5, :])
            s_ln.inc(i)
            v.wait_ge(s_ln.h, 5)
            i = v.tensor_mul(out=rows[:, 4, :], in0=rows[:, 0, :], in1=rows[:, 3, :])
            s_ln.inc(i)
            v.wait_ge(s_ln.h, 6)
            v.tensor_copy(out=rows_bf[:, 0, :], in_=rows[:, 3, :])
            i = v.tensor_copy(out=rows_bf[:, 1, :], in_=rows[:, 4, :])
            s_ln.inc(i)
            for c in range(2):
                cs = slice(c * 512, (c + 1) * 512)
                v.wait_ge(s_bc.h, c + 1)
                for et in range(16):
                    par = et % 2
                    if c == 1 and et >= 2:
                        v.wait_ge(s_y[par].h, 16 * (et // 2))
                    v.wait_ge(s_nm.h, s_nm.v)
                    i = v.tensor_mul(out=tmp_f[:, :], in0=x_s[:, et, cs], in1=ps_r[:, 0, :])
                    s_nm.inc(i)
                    v.wait_ge(s_nm.h, s_nm.v)
                    i = v.tensor_sub(out=x_s[:, et, cs], in0=tmp_f[:, :], in1=ps_r[:, 1, :])
                    s_nm.inc(i)

        @block.scalar
        def _(sc):
            for tt in range(8):
                B0 = 6 + 10 * tt
                sc.wait_ge(s_pp.h, B0 + 5)
                i = sc.activation(out=v12[:, 0:1], in_=m12[:, 2:3], func=AF.Sigmoid,
                                  bias=0.0, scale=1.0)
                i.then_inc(s_pp.h, 1)
            sc.wait_ge(s_ln.h, 1)
            i = sc.square(out=rows[:, 5, :], in_=rows[:, 0, :])
            s_ln.inc(i)
            sc.wait_ge(s_ln.h, 3)
            i = sc.activation(out=rows[:, 5, :], in_=rows[:, 2, :], func=AF.Sqrt,
                              bias=eps_t[0:1, 0:1], scale=1.0)
            s_ln.inc(i)

        @block.tensor
        def _(te):
            te.wait_ge(s_pp.h, 4)
            for dt in range(16):
                te.matmul(ps_a[0:8, :], rw_bf[:, dt, :], x_bf[:, dt, 0:512],
                          start=(dt == 0), stop=(dt == 15))
                i = te.matmul(ps_b[0:8, :], rw_bf[:, dt, :], x_bf[:, dt, 512:TPC],
                              start=(dt == 0), stop=(dt == 15))
            i.then_inc(s_pp.h, 1)
            for tt in range(8):
                te.wait_ge(s_pp.h, 6)
                if tt >= 1:
                    te.wait_ge(s_tpv.h, 2 * (tt - 1) + 1)
                i = te.transpose(ps_t1[:, :], lg_bf[:, tt * 128:(tt + 1) * 128],
                                 eye_s[0:8, 0:8])
                i.then_inc(s_tpe.h, 1)
                te.wait_ge(s_pp.h, 6 + 10 * (tt + 1))
                if tt >= 1:
                    te.wait_ge(s_tpv.h, 2 * (tt - 1) + 2)
                i = te.transpose(ps_t2[:, :], wtok[:, :], eye_s[:])
                i.then_inc(s_tpe.h, 1)
            te.wait_ge(s_tpv.h, 16)
            te.matmul(ps_a[:, :], rp_s[:, 0:128], w8[0:8, 0:512], start=True, stop=True)
            i = te.matmul(ps_b[:, :], rp_s[:, 128:256], w8[0:8, 0:512], start=True, stop=True)
            i.then_inc(s_pp.h, 1)
            te.wait_ge(s_pp.h, TOPK_END + 2)
            te.matmul(ps_a[:, :], rp_s[:, 0:128], w8[0:8, 512:TPC], start=True, stop=True)
            i = te.matmul(ps_b[:, :], rp_s[:, 128:256], w8[0:8, 512:TPC], start=True, stop=True)
            i.then_inc(s_pp.h, 1)
            k = 0
            for bt in range(2):
                for c in range(2):
                    cs = slice(c * 512, (c + 1) * 512)
                    te.wait_ge(s_pp.h, TOPK_END + 4 + 2 * k)
                    psx = ps_a if c == 0 else ps_b
                    for dt in range(16):
                        i = te.matmul(psx[:, :], apad_bf[:, dt, bt * 128:(bt + 1) * 128],
                                      x_bf[:, dt, cs], start=(dt == 0), stop=(dt == 15))
                    i.then_inc(s_pp.h, 1)
                    k += 1
            for et in range(16):
                par = et % 2
                te.wait_ge(s_cast.h, et + 1)
                if et == 0:
                    te.wait_ge(s_pp.h, PRE_MAIN)
                else:
                    te.wait_ge(s_ra.h, et)
                for dt in range(16):
                    te.matmul(ps_a[:, :], wt_b[par][:, dt, :], x_bf[:, dt, 0:512],
                              start=(dt == 0), stop=False)
                    te.matmul(ps_b[:, :], wt_b[par][:, dt, :], x_bf[:, dt, 512:TPC],
                              start=(dt == 0), stop=False)
                for bt in range(2):
                    te.matmul(ps_a[:, :], bst_bf[:, bt, et * 128:(et + 1) * 128],
                              gbuf[:, bt, 0:512], start=False, stop=(bt == 1))
                    i = te.matmul(ps_b[:, :], bst_bf[:, bt, et * 128:(et + 1) * 128],
                                  gbuf[:, bt, 512:TPC], start=False, stop=(bt == 1))
                s_mm.inc(i)
                te.wait_ge(s_sq.h, et + 1)
                for c in range(2):
                    cs = slice(c * 512, (c + 1) * 512)
                    te.wait_ge(s_sa.h, 2 * et + c)
                    te.matmul(ps_s1[:, :], ones_cf[:, :], x_s[:, et, cs],
                              start=True, stop=True)
                    i = te.matmul(ps_s2[:, :], ones_c[:, :], sq_t[:, cs],
                                  start=True, stop=True)
                    s_st.inc(i)
            for c in range(2):
                cs = slice(c * 512, (c + 1) * 512)
                te.wait_ge(s_ln.h, 7)
                if c == 1:
                    te.wait_ge(s_nm.h, 32)
                te.matmul(ps_r[:, 0, :], ones_r[0:1, 0:128], rows_bf[0:1, 0, cs],
                          start=True, stop=True)
                i = te.matmul(ps_r[:, 1, :], ones_r[0:1, 0:128], rows_bf[0:1, 1, cs],
                              start=True, stop=True)
                s_bc.inc(i)

        @block.gpsimd
        def _(g):
            for et in range(16):
                par = et % 2
                g.wait_ge(s_nm.h, 32 + 2 * (et + 1))
                g.dma_start(out=yT[et * 128:(et + 1) * 128, :], in_=x_s[:, et, :]
                            ).then_inc(s_y[par].h, 16)
                s_y[par].v += 16
            g.wait_ge(s_y[0].h, s_y[0].v)
            g.wait_ge(s_y[1].h, s_y[1].v)
    return nc


def _run(nc, in_maps):
    res = run_bass_kernel_spmd(nc, in_maps, core_ids=list(range(NCORE)))
    return res.results


def _qkv_maps(x_by_core, w, b_t, rope, eyeblk):
    maps = []
    for c in range(NCORE):
        m = {"xT": x_by_core[c], "w": w, "b_t": b_t, "eyeblk": eyeblk}
        if rope:
            cons = host_constants(c)
            m["rope_c"] = cons["rope_c"]
            m["rope_s"] = cons["rope_s"]
        maps.append(m)
    return maps


def _attn_maps(qkv_res, layer):
    NHL = 8 if layer == 0 else 4
    DH = 64 if layer == 0 else 128
    t01 = np.ascontiguousarray(
        (np.arange(128)[None, :] >= np.arange(128)[:, None]).astype(BF))
    eye = np.eye(128, dtype=BF)
    maps = []
    for c in range(NCORE):
        seq, mem = c // 4, c % 4
        members = [seq * 4 + mm for mm in range(4)]
        heads = range(mem * NHL, (mem + 1) * NHL)
        Qh = np.empty((512, S), BF)
        Kh = np.empty((512, S), BF)
        Vh = np.empty((NHL * S, DH), BF)
        qnh = np.empty((NHL, S), np.float32)
        knh = np.empty((NHL, S), np.float32)
        for hi, gh in enumerate(heads):
            rs = slice(gh * DH, (gh + 1) * DH)
            for mi, mc in enumerate(members):
                ts = slice(mi * TPC, (mi + 1) * TPC)
                Kh[hi * DH:(hi + 1) * DH, ts] = qkv_res[mc]["kT"][rs]
                Vh[hi * S + mi * TPC:hi * S + (mi + 1) * TPC, :] = qkv_res[mc]["vT"][rs].T
                if layer == 0:
                    row, tile = gh % 2, gh // 2
                else:
                    row, tile = 0, gh
                knh[hi, ts] = qkv_res[mc]["kn"][row, tile * TPC:(tile + 1) * TPC]
                if layer == 0:
                    Qh[hi * DH:(hi + 1) * DH, ts] = qkv_res[mc]["qT"][rs]
                    qnh[hi, ts] = qkv_res[mc]["qn"][row, tile * TPC:(tile + 1) * TPC]
            if layer == 1:
                mq = gh // 4
                qsrc = qkv_res[seq * 4 + mq]["qT"]
                ub = (gh % 4) * 256
                Bq = qsrc.reshape(16, 128, TPC)[:, :, ub:ub + 256]
                Qh[hi * DH:(hi + 1) * DH, :] = np.ascontiguousarray(
                    Bq.transpose(1, 2, 0)).reshape(DH, S)
                qn_src = qkv_res[seq * 4 + mq]["qn"]
                qn_rd = qn_src[0].reshape(16, TPC)[:, ub:ub + 256]
                qnh[hi, :] = qn_rd.T.reshape(S)
        maps.append({"Qh": Qh, "Kh": Kh, "Vh": np.ascontiguousarray(Vh),
                     "qnh": qnh, "knh": knh, "t01": t01, "eye": eye})
    return maps


def _o_maps(attn_res, layer):
    NHL = 8 if layer == 0 else 4
    DH = 64 if layer == 0 else 128
    NH = 32 if layer == 0 else 16
    outs = []
    for c in range(NCORE):
        seq, mem = c // 4, c % 4
        ts = slice(mem * TPC, (mem + 1) * TPC)
        oT = np.empty((D, TPC), BF)
        for gh in range(NH):
            ac = seq * 4 + gh // NHL
            hl = gh % NHL
            oT[gh * DH:(gh + 1) * DH, :] = attn_res[ac]["Oh"][hl * DH:(hl + 1) * DH, ts]
        outs.append(np.ascontiguousarray(oT))
    return outs


def kernel(**inputs):
    I = {k: np.asarray(v) for k, v in inputs.items()}
    x = I["L0_attn_x"]

    wgT, wuT, wdT = run_ghost(I)

    x_by_core = [np.ascontiguousarray(x[c // 4, (c % 4) * TPC:(c % 4 + 1) * TPC].T)
                 for c in range(NCORE)]
    eyeblk0 = np.zeros((128, 2), BF)
    eyeblk0[0:64, 0] = 1
    eyeblk0[64:128, 1] = 1
    b_t0 = np.ascontiguousarray(I["L0_attn_bqkv"].reshape(48, 128).T)
    nc = build_qkv(True, 0.125, 2)
    qkv0 = _run(nc, _qkv_maps(x_by_core, np.ascontiguousarray(I["L0_attn_Wqkv"]),
                              b_t0, True, eyeblk0))

    nc = build_attn(8, 64)
    at0 = _run(nc, _attn_maps(qkv0, 0))

    o0 = _o_maps(at0, 0)
    nc = build_postln(True)
    w_out = np.ascontiguousarray(I["L0_attn_Wout"])
    b_out = np.ascontiguousarray(I["L0_attn_bout"].reshape(1, D))
    pl0 = _run(nc, [{"oT": o0[c], "xT": x_by_core[c], "w": w_out, "b_row": b_out}
                    for c in range(NCORE)])
    y1 = [np.ascontiguousarray(pl0[c]["yT"]) for c in range(NCORE)]

    nc = build_ffn0()
    f0 = _run(nc, [{"xT": y1[c], "wgT": wgT, "wuT": wuT, "wdT": wdT}
                   for c in range(NCORE)])
    y2 = [np.ascontiguousarray(f0[c]["yT"]) for c in range(NCORE)]

    w1 = np.ascontiguousarray(np.concatenate(
        [I["L1_attn_Wq"], I["L1_attn_Wk"], I["L1_attn_Wv"]], axis=1))
    b1 = np.concatenate([I["L1_attn_bq"], I["L1_attn_bk"], I["L1_attn_bv"]])
    b_t1 = np.ascontiguousarray(b1.reshape(48, 128).T)
    eyeblk1 = np.ones((128, 1), BF)
    nc = build_qkv(False, float(1.0 / np.sqrt(128.0)), 1)
    qkv1 = _run(nc, _qkv_maps(y2, w1, b_t1, False, eyeblk1))

    nc = build_attn(4, 128)
    at1 = _run(nc, _attn_maps(qkv1, 1))

    o1 = _o_maps(at1, 1)
    nc = build_postln(True)
    w_o = np.ascontiguousarray(I["L1_attn_Wo"])
    b_o = np.ascontiguousarray(I["L1_attn_bo"].reshape(1, D))
    pl1 = _run(nc, [{"oT": o1[c], "xT": y2[c], "w": w_o, "b_row": b_o}
                    for c in range(NCORE)])
    y3 = [np.ascontiguousarray(pl1[c]["yT"]) for c in range(NCORE)]

    A = I["L1_ffn_lora_A"]
    apadT = np.zeros((D, 256), np.float32)
    for e in range(8):
        apadT[:, e * 32:e * 32 + 16] = A[e].T
    Bm = I["L1_ffn_lora_B"]
    bstack = np.zeros((256, D), np.float32)
    for e in range(8):
        bstack[e * 32:e * 32 + 16, :] = Bm[e].T
    Rpm = np.zeros((8, 256), BF)
    for e in range(8):
        Rpm[e, e * 32:(e + 1) * 32] = 1
    nc = build_ffn1()
    f1 = _run(nc, [{"xT": y3[c], "w": np.ascontiguousarray(I["L1_ffn_weight"].T),
                    "apadT": apadT, "bstack": bstack,
                    "rwT": np.ascontiguousarray(I["L1_ffn_router_w"].T),
                    "Rp": Rpm, "eye": np.eye(128, dtype=BF)}
                   for c in range(NCORE)])

    out = np.empty((B, S, D), np.float32)
    for c in range(NCORE):
        seq, mem = c // 4, c % 4
        out[seq, mem * TPC:(mem + 1) * TPC, :] = f1[c]["yT"].T
    return out
